# version 02 (final)
# baseline (speedup 1.0000x reference)
from contextlib import ExitStack

import numpy as np
import ml_dtypes

import concourse.bass as bass
import concourse.mybir as mybir
from concourse.bass_utils import run_bass_kernel_spmd

F32 = mybir.dt.float32
BF16 = mybir.dt.bfloat16
U8 = mybir.dt.uint8
AF = mybir.ActivationFunctionType
ALU = mybir.AluOpType
AX = mybir.AxisListType
BF = ml_dtypes.bfloat16

E = 4096
S = 8192
NB = 2
NC = 8
TOWN = 2048
DH = 256
ROT = 64
NEG = -1e9
EXP_SHIFT = -40.0
II = 11008



class Chan:

    def __init__(self, nc, ctx, name):
        self.sem = ctx.enter_context(nc.semaphore(name))
        self.n = 0


class _Eng:
    def __init__(self, prog, name):
        self.prog = prog
        self.name = name
        self.sem = None
        self.n = 0

    def op(self, fn, inc=False):
        if inc:
            self.n += 1
            self.prog.q[self.name].append(
                lambda e, f=fn, h=self.sem: f(e).then_inc(h, 1))
            return self.n
        self.prog.q[self.name].append(fn)
        return None

    def wait(self, eng_or_sem, val):
        h = eng_or_sem.sem if isinstance(eng_or_sem, _Eng) else eng_or_sem
        self.prog.q[self.name].append(lambda e, h=h, v=val: e.wait_ge(h, v))

    def wait_chan(self, chan):
        self.prog.q[self.name].append(
            lambda e, h=chan.sem, v=chan.n: e.wait_ge(h, v))

    def dma(self, chan, dst, src):
        chan.n += 16
        self.prog.q[self.name].append(
            lambda e, d=dst, s=src, h=chan.sem: e.dma_start(out=d, in_=s).then_inc(h, 16))
        return chan.n


class Prog:
    ENGS = ("tensor", "vector", "scalar", "gpsimd", "sync")

    def __init__(self, nc, ctx):
        self.nc = nc
        self.ctx = ctx
        self.q = {e: [] for e in self.ENGS}
        self.t = _Eng(self, "tensor")
        self.v = _Eng(self, "vector")
        self.c = _Eng(self, "scalar")
        self.g = _Eng(self, "gpsimd")
        self.y = _Eng(self, "sync")
        for e in (self.t, self.v, self.c, self.g, self.y):
            e.sem = ctx.enter_context(nc.semaphore(f"s_{e.name}"))

    def chan(self, name):
        return Chan(self.nc, self.ctx, name)

    def run(self):
        with self.nc.Block() as block:
            for eng in self.ENGS:
                fns = self.q[eng]
                if fns:
                    getattr(block, eng)(lambda e, fns=fns: [f(e) for f in fns])



def _orig_map(N):
    n = np.arange(N)
    bt, r = n // 256, n % 256
    half, rr = r // 128, r % 128
    j, b = rr // 4, rr % 4
    lo_m = np.array([0, 4, 1, 5])
    hi_m = np.array([2, 6, 3, 7])
    m = np.where(half == 0, lo_m[b], hi_m[b])
    return bt * 256 + j * 8 + m


def _pack_bytes(qw):
    K, W = qw.shape
    return np.ascontiguousarray(
        np.ascontiguousarray(qw).view("<u1").reshape(K, W, 4).transpose(1, 2, 0)
    ).reshape(W * 4, K)


def _consts():
    c = {}
    c["ident_bf"] = np.eye(128, dtype=BF)
    c["ident_f"] = np.eye(128, dtype=np.float32)
    ps = np.zeros((64, 64), dtype=np.float32)
    for d in range(64):
        ps[d ^ 1, d] = 1.0
    c["pswap"] = ps.astype(BF)
    c["ones_col_bf"] = np.ones((128, 1), dtype=BF)
    c["ones_row_f"] = np.ones((1, 128), dtype=BF)
    r4 = np.zeros((4, 64), dtype=BF)
    for ee in range(4):
        r4[ee, ee * 16:(ee + 1) * 16] = 1.0
    c["r4"] = r4
    inv = 1.0 / 10000.0 ** (np.arange(0, ROT, 2, dtype=np.float32) / ROT)
    ang = np.arange(S, dtype=np.float32)[None, :] * np.repeat(inv, 2)[:, None]
    c["rsincos"] = np.concatenate(
        [np.sin(ang), np.cos(ang)], axis=0).astype(BF)
    m = np.zeros((4, 128, 512), dtype=np.float32)
    pp = np.arange(128)[:, None]
    jj = np.arange(512)[None, :]
    for mi in range(4):
        m[mi] = np.where(pp > jj - 128 * mi, NEG, 0.0)
    c["masks"] = m.reshape(4 * 128, 512).astype(BF)
    b_of_m = [0, 2, 0, 2, 1, 3, 1, 3]
    half_of_m = [0, 0, 1, 1, 0, 0, 1, 1]
    P = np.zeros((2, 2, 128, 128), dtype=BF)
    for q in range(256):
        j8, m8 = q // 8, q % 8
        rp = 4 * j8 + b_of_m[m8]
        P[half_of_m[m8], q // 128, rp, q % 128] = 1.0
    c["punp"] = np.ascontiguousarray(P.transpose(1, 0, 2, 3).reshape(4 * 128, 128))
    return c



def build_ln(ntok):
    nc = bass.Bass("TRN2", target_bir_lowering=False, detect_race_conditions=False,
                   dynamic_dma_scratch_size=16384)
    x = nc.dram_tensor("x", [ntok, E], F32, kind="ExternalInput").ap()
    idb = nc.dram_tensor("ident_bf", [128, 128], BF16, kind="ExternalInput").ap()
    xnT = nc.dram_tensor("xnT", [E, ntok], BF16, kind="ExternalOutput")

    nt = ntok // 128
    with ExitStack() as ctx:
        en = ctx.enter_context
        sx = [en(nc.sbuf_tensor(f"sx{i}", [128, E], F32)) for i in range(2)]
        sxc = en(nc.sbuf_tensor("sxc", [128, E], F32))
        ssq = en(nc.sbuf_tensor("ssq", [128, E], F32))
        sm = en(nc.sbuf_tensor("sm", [128, 1], F32))
        svv = en(nc.sbuf_tensor("svv", [128, 1], F32))
        srs = en(nc.sbuf_tensor("srs", [128, 1], F32))
        sxn = en(nc.sbuf_tensor("sxn", [128, E], BF16))
        sid = en(nc.sbuf_tensor("sid", [128, 128], BF16))
        stripe = en(nc.sbuf_tensor("stripe", [128, E], BF16))
        pt = [en(nc.psum_tensor(f"pt{i}", [128, 512], BF16)) for i in range(2)]
        seps = en(nc.sbuf_tensor("seps", [128, 1], F32))
        p = Prog(nc, ctx)
        ch_x = [p.chan("cx0"), p.chan("cx1")]
        ch_c = p.chan("cc")
        ch_st = p.chan("cst")

        ng0 = p.g.op(lambda e: e.memset(seps[:, :], 1e-5), inc=True)
        p.c.wait(p.g, ng0)
        p.y.dma(ch_c, sid[:, :], idb)
        last_nv = [0, 0]
        for i in range(nt):
            buf = sx[i % 2]
            ch = ch_x[i % 2]
            if i >= 2:
                p.y.wait(p.v, last_nv[i % 2])
            p.y.dma(ch, buf[:, :], x[i * 128:(i + 1) * 128, :])
            p.v.wait_chan(ch)
            nr = p.v.op(lambda e, b=buf: e.reduce_sum(out=sm[:, :], in_=b[:, :],
                                                      axis=AX.X), inc=True)
            p.v.wait(p.v, nr)
            nr = p.v.op(lambda e: e.tensor_scalar(out=sm[:, :], in0=sm[:, :],
                                                  scalar1=1.0 / E, scalar2=None,
                                                  op0=ALU.mult), inc=True)
            p.v.wait(p.v, nr)
            nv = p.v.op(lambda e, b=buf: e.tensor_scalar(
                out=sxc[:, :], in0=b[:, :], scalar1=sm[:, :], scalar2=None,
                op0=ALU.subtract), inc=True)
            last_nv[i % 2] = nv
            p.c.wait(p.v, nv)
            ncs = p.c.op(lambda e: e.activation(out=ssq[:, :], in_=sxc[:, :],
                                                func=AF.Square, accum_out=svv[:, :]),
                         inc=True)
            p.c.wait(p.c, ncs)
            ncs = p.c.op(lambda e: e.activation(out=srs[:, :], in_=svv[:, :], func=AF.Ln,
                                                scale=1.0 / E, bias=seps[:, :]), inc=True)
            p.c.wait(p.c, ncs)
            ncc = p.c.op(lambda e: e.activation(out=srs[:, :], in_=srs[:, :],
                                                func=AF.Exp, scale=-0.5), inc=True)
            p.v.wait(p.c, ncc)
            if i >= 1:
                p.v.wait(p.t, p.t.n)
            nv2 = p.v.op(lambda e: e.tensor_scalar(
                out=sxn[:, :], in0=sxc[:, :], scalar1=srs[:, :], scalar2=None,
                op0=ALU.mult), inc=True)
            p.t.wait(p.v, nv2)
            if i == 0:
                p.t.wait_chan(ch_c)
            if i >= 1:
                p.v.wait_chan(ch_st)
            ev = [0] * 8
            for g in range(8):
                bank = pt[g % 2]
                if g >= 2:
                    p.t.wait(p.v, ev[g - 2])
                tv = None
                for k4 in range(4):
                    et = g * 4 + k4
                    tv = p.t.op(lambda e, b=bank, k4=k4, et=et: e.transpose(
                        out=b[:, k4 * 128:(k4 + 1) * 128],
                        in_=sxn[:, et * 128:(et + 1) * 128],
                        identity=sid[:, :]), inc=(k4 == 3))
                p.v.wait(p.t, tv)
                ev[g] = p.v.op(lambda e, b=bank, g=g: e.tensor_copy(
                    out=stripe[:, g * 512:(g + 1) * 512], in_=b[:, :]), inc=True)
            p.y.wait(p.v, p.v.n)
            dst = xnT.ap().rearrange("(a p) t -> p a t", p=128)[:, :, i * 128:(i + 1) * 128]
            src = stripe.ap().rearrange("p (a c) -> p a c", c=128)
            p.y.dma(ch_st, dst, src)
        p.y.wait_chan(ch_st)
        p.run()
    return nc



def build_attn(SL=S, nheads=4, detect=False, dbg=False, attn_phase=True, nonorm=False, noexp=False, nostore=False, nobcast=False, attn_upto=99, nohw=0):
    nc = bass.Bass("TRN2", target_bir_lowering=False, detect_race_conditions=detect,
                   dynamic_dma_scratch_size=16384)
    xnT = nc.dram_tensor("xnT", [E, SL], BF16, kind="ExternalInput").ap()
    wq = nc.dram_tensor("wq", [E, 4 * DH], F32, kind="ExternalInput").ap()
    wk = nc.dram_tensor("wk", [E, 4 * DH], F32, kind="ExternalInput").ap()
    wv = nc.dram_tensor("wv", [E, 4 * DH], F32, kind="ExternalInput").ap()
    rsc = nc.dram_tensor("rsincos", [128, SL], BF16, kind="ExternalInput").ap()
    masks = nc.dram_tensor("masks", [4 * 128, 512], BF16, kind="ExternalInput").ap()
    pswap = nc.dram_tensor("pswap", [64, 64], BF16, kind="ExternalInput").ap()
    onesc = nc.dram_tensor("ones_col_bf", [128, 1], BF16, kind="ExternalInput").ap()
    onesr = nc.dram_tensor("ones_row_f", [1, 128], BF16, kind="ExternalInput").ap()
    attnpre = nc.dram_tensor("attnpre", [4 * DH, SL], BF16, kind="ExternalOutput")
    dbgq = nc.dram_tensor("dbgq", [128, 2 * SL], BF16, kind="ExternalOutput") if dbg else None
    dbgk = nc.dram_tensor("dbgk", [128, 2 * SL], BF16, kind="ExternalOutput") if dbg else None

    NTT = SL // 256
    NQT = SL // 512
    NET = E // 128
    with ExitStack() as ctx:
        en = ctx.enter_context
        qT = en(nc.sbuf_tensor("qT", [128, 2 * SL], BF16))
        kT = en(nc.sbuf_tensor("kT", [128, 2 * SL], BF16))
        v16 = en(nc.sbuf_tensor("v16", [128, (SL // 128) * DH], BF16))
        sst = en(nc.sbuf_tensor("sst", [64, 512], BF16))
        smask = en(nc.sbuf_tensor("smask", [128, 4 * 512], BF16))
        spsw = en(nc.sbuf_tensor("spsw", [64, 64], BF16))
        q0b = en(nc.sbuf_tensor("q0b", [64, 256], BF16))
        sonesc = en(nc.sbuf_tensor("sonesc", [128, 1], BF16))
        sonesr = en(nc.sbuf_tensor("sonesr", [1, 128], BF16))
        sshift = en(nc.sbuf_tensor("sshift", [128, 1], F32))
        w16 = {k: en(nc.sbuf_tensor(f"w16{k}", [128, NET * DH], BF16)) for k in "qkv"}
        xt = [en(nc.sbuf_tensor(f"xt{i}", [128, NET * 256], BF16)) for i in range(2)]
        q0f = en(nc.sbuf_tensor("q0f", [64, 256], F32))
        sm1 = en(nc.sbuf_tensor("sm1", [64, 256], F32))
        sm2 = en(nc.sbuf_tensor("sm2", [64, 256], F32))
        aT = [en(nc.sbuf_tensor(f"aT{i}", [128, 512], BF16)) for i in range(3)]
        rinv = en(nc.sbuf_tensor("rinv", [1, 512], F32))
        rinvb = en(nc.sbuf_tensor("rinvb", [1, 512], BF16))
        rb = en(nc.sbuf_tensor("rb", [128, 512], F32))
        oT = en(nc.sbuf_tensor("oT", [128, 2 * 512], BF16))
        p = Prog(nc, ctx)
        ch_cst = p.chan("c_cst")
        ch_w = p.chan("c_w")
        ch_xt = [p.chan("c_xt0"), p.chan("c_xt1")]
        ch_sst = p.chan("c_sst")
        ch_oT = p.chan("c_oT")

        p.y.dma(ch_cst, spsw[:, :], pswap)
        p.y.dma(ch_cst, sonesc[:, :], onesc)
        p.y.dma(ch_cst, sonesr[:, :], onesr)
        p.y.dma(ch_cst, smask[:, :], masks.rearrange("(a p) j -> p a j", p=128))
        p.g.op(lambda e: e.memset(sshift[:, :], EXP_SHIFT), inc=True)
        gshift = p.g.n

        prev_attn = [0, 0]
        mm_cnt = [0, 0]
        sst_rd = [0]
        for h in range(nheads):
            if h < (99 if not nohw else nohw):
                for k, w in (("q", wq), ("k", wk), ("v", wv)):
                    p.g.dma(ch_w, w16[k][:, :].rearrange("p (a d) -> p a d", d=DH),
                            w[:, h * DH:(h + 1) * DH].rearrange("(a p) d -> p a d", p=128))
            with ExitStack() as pctx:
                pe = pctx.enter_context
                pq = [pe(nc.psum_tensor(f"pq{i}_{h}", [128, 512], F32)) for i in range(2)]
                pk = [pe(nc.psum_tensor(f"pk{i}_{h}", [128, 512], F32)) for i in range(2)]
                pv = [pe(nc.psum_tensor(f"pv{i}_{h}", [128, 512], F32)) for i in range(2)]
                psw = pe(nc.psum_tensor(f"psw_{h}", [64, 256], F32))
                p.t.wait_chan(ch_w)
                p.t.wait_chan(ch_cst)
                if h >= 1:
                    p.t.wait(p.v, prev_attn[0])
                    p.t.wait(p.c, prev_attn[1])
                evac_cnt = [0, 0]
                for tt in range(NTT):
                    xb = xt[tt % 2]
                    if mm_cnt[tt % 2] > 0:
                        p.y.wait(p.t, mm_cnt[tt % 2])
                    if sst_rd[0] > 0:
                        p.y.wait(p.v, sst_rd[0])
                    p.y.dma(ch_sst, sst[:, 0:256], rsc[0:64, tt * 256:(tt + 1) * 256])
                    p.y.dma(ch_sst, sst[:, 256:512], rsc[64:128, tt * 256:(tt + 1) * 256])
                    p.y.dma(ch_xt[tt % 2], xb[:, :].rearrange("p (a d) -> p a d", d=256),
                            xnT[:, tt * 256:(tt + 1) * 256]
                            .rearrange("(a p) t -> p a t", p=128))
                    p.t.wait_chan(ch_xt[tt % 2])
                    p.v.wait_chan(ch_sst)
                    if tt >= 2:
                        p.t.wait(p.c, evac_cnt[tt % 2])
                    par = tt % 2
                    for dr in range(2):
                        for et in range(NET):
                            p.t.op(lambda e, dr=dr, et=et, xb=xb, par=par: e.matmul(
                                out=pq[par][:, dr * 256:(dr + 1) * 256],
                                lhsT=w16["q"][:, et * DH + dr * 128: et * DH + dr * 128 + 128],
                                rhs=xb[:, et * 256:(et + 1) * 256],
                                start=(et == 0), stop=(et == NET - 1)))
                        for et in range(NET):
                            p.t.op(lambda e, dr=dr, et=et, xb=xb, par=par: e.matmul(
                                out=pk[par][:, dr * 256:(dr + 1) * 256],
                                lhsT=w16["k"][:, et * DH + dr * 128: et * DH + dr * 128 + 128],
                                rhs=xb[:, et * 256:(et + 1) * 256],
                                start=(et == 0), stop=(et == NET - 1)))
                    ntm = None
                    for ts in range(2):
                        for et in range(NET):
                            ntm = p.t.op(lambda e, ts=ts, et=et, xb=xb, par=par: e.matmul(
                                out=pv[par][:, ts * 256:(ts + 1) * 256],
                                lhsT=xb[:, et * 256 + ts * 128: et * 256 + ts * 128 + 128],
                                rhs=w16["v"][:, et * DH:(et + 1) * DH],
                                start=(et == 0), stop=(et == NET - 1)),
                                inc=(ts == 1 and et == NET - 1))
                    mm_cnt[tt % 2] = ntm
                    p.c.wait(p.t, ntm)
                    c0 = tt * 256
                    for dr in range(2):
                        p.c.op(lambda e, dr=dr, c0=c0, par=par: e.activation(
                            out=qT[:, SL + c0:SL + c0 + 256] if dr else qT[64:128, c0:c0 + 256],
                            in_=pq[par][:, 256:512] if dr else pq[par][64:128, 0:256],
                            func=AF.Identity))
                        p.c.op(lambda e, dr=dr, c0=c0, par=par: e.activation(
                            out=kT[:, SL + c0:SL + c0 + 256] if dr else kT[64:128, c0:c0 + 256],
                            in_=pk[par][:, 256:512] if dr else pk[par][64:128, 0:256],
                            func=AF.Identity))
                    for ts in range(2):
                        p.c.op(lambda e, ts=ts, par=par, tt=tt: e.activation(
                            out=v16[:, (tt * 2 + ts) * 256:(tt * 2 + ts + 1) * 256],
                            in_=pv[par][:, ts * 256:(ts + 1) * 256], func=AF.Identity))
                    for which in ("q", "k"):
                        pq0 = pq[par] if which == "q" else pk[par]
                        dstT = qT if which == "q" else kT
                        p.c.op(lambda e, pq0=pq0: e.activation(
                            out=q0f[:, :], in_=pq0[0:64, 0:256], func=AF.Identity))
                        ncp = p.c.op(lambda e, pq0=pq0: e.activation(
                            out=q0b[:, :], in_=pq0[0:64, 0:256], func=AF.Identity), inc=True)
                        p.t.wait(p.c, ncp)
                        ntw = p.t.op(lambda e: e.matmul(
                            out=psw[:, :], lhsT=spsw[:, :], rhs=q0b[:, :],
                            start=True, stop=True), inc=True)
                        p.v.wait(p.c, ncp)
                        nv = p.v.op(lambda e: e.tensor_tensor(
                            out=sm1[:, :], in0=q0f[:, :], in1=sst[:, 256:512],
                            op=ALU.mult), inc=True)
                        p.v.wait(p.t, ntw)
                        p.v.wait(p.v, nv)
                        nv = p.v.op(lambda e: e.tensor_tensor(
                            out=sm2[:, :], in0=psw[:, :], in1=sst[:, 0:256],
                            op=ALU.mult), inc=True)
                        p.v.wait(p.v, nv)
                        nv = p.v.op(lambda e, dstT=dstT, c0=c0: e.tensor_tensor(
                            out=dstT[0:64, c0:c0 + 256], in0=sm1[:, :], in1=sm2[:, :],
                            op=ALU.add), inc=True)
                        p.t.wait(p.v, nv)
                        p.c.wait(p.v, nv)
                        sst_rd[0] = nv
                    evac_cnt[tt % 2] = p.c.n
            if (not attn_phase) or h >= attn_upto:
                continue
            with ExitStack() as pctx:
                pe = pctx.enter_context
                ps = [pe(nc.psum_tensor(f"ps{i}_{h}", [128, 512], F32)) for i in range(2)]
                po = [pe(nc.psum_tensor(f"po{i}_{h}", [128, 512], F32)) for i in range(2)]
                pr = pe(nc.psum_tensor(f"pr_{h}", [1, 512], F32))
                pb = pe(nc.psum_tensor(f"pb_{h}", [128, 512], F32))
                p.c.wait(p.g, gshift)
                p.v.wait_chan(ch_cst)
                last_nvo = [0]
                for qt in range(NQT):
                    nk = 4 * qt + 4
                    q0 = qt * 512
                    exp_cnt = [0] * nk
                    av_cnt = [0] * nk
                    for kt in range(nk):
                        bank = ps[kt % 2]
                        if kt >= 2:
                            p.t.wait(p.c, exp_cnt[kt - 2])
                        nts = None
                        for dr in range(2):
                            nts = p.t.op(lambda e, dr=dr, kt=kt, q0=q0, bank=bank: e.matmul(
                                out=bank[:, :],
                                lhsT=kT[:, dr * SL + kt * 128: dr * SL + kt * 128 + 128],
                                rhs=qT[:, dr * SL + q0: dr * SL + q0 + 512],
                                start=(dr == 0), stop=(dr == 1)), inc=(dr == 1))
                        mi = kt - 4 * qt
                        if mi >= 0:
                            p.v.wait(p.t, nts)
                            nvm = p.v.op(lambda e, bank=bank, mi=mi: e.tensor_tensor(
                                out=bank[:, :], in0=bank[:, :],
                                in1=smask[:, mi * 512:(mi + 1) * 512], op=ALU.add), inc=True)
                            p.c.wait(p.v, nvm)
                        else:
                            p.c.wait(p.t, nts)
                        if kt >= 3:
                            p.c.wait(p.t, av_cnt[kt - 3])
                        ab = aT[kt % 3]
                        nce = p.c.op(lambda e, bank=bank, ab=ab: e.activation(
                            out=ab[:, :], in_=bank[:, :],
                            func=AF.Identity if noexp else AF.Exp,
                            scale=1.0 / 16, bias=sshift[:, :]), inc=True)
                        exp_cnt[kt] = nce
                        p.t.wait(p.c, nce)
                        if kt == 0:
                            p.t.wait(p.v, last_nvo[0])
                        p.t.op(lambda e, ab=ab, kt=kt, nk=nk: e.matmul(
                            out=pr[:, :], lhsT=sonesc[:, :], rhs=ab[:, :],
                            start=(kt == 0), stop=(kt == nk - 1)))
                        p.t.op(lambda e, ab=ab, kt=kt, nk=nk: e.matmul(
                            out=po[0][:, :], lhsT=v16[:, kt * 256: kt * 256 + 128],
                            rhs=ab[:, :], start=(kt == 0), stop=(kt == nk - 1)))
                        av_cnt[kt] = p.t.op(lambda e, ab=ab, kt=kt, nk=nk: e.matmul(
                            out=po[1][:, :], lhsT=v16[:, kt * 256 + 128: kt * 256 + 256],
                            rhs=ab[:, :], start=(kt == 0), stop=(kt == nk - 1)), inc=True)
                    if nonorm:
                        continue
                    p.v.wait(p.t, av_cnt[nk - 1])
                    nvr = p.v.op(lambda e: e.reciprocal(out=rinv[:, :], in_=pr[:, :]),
                                 inc=True)
                    p.v.wait(p.v, nvr)
                    nvr = p.v.op(lambda e: e.tensor_copy(out=rinvb[:, :], in_=rinv[:, :]),
                                 inc=True)
                    p.t.wait(p.v, nvr)
                    if nobcast:
                        ntb = p.t.op(lambda e: e.nop(), inc=True)
                    else:
                        ntb = p.t.op(lambda e: e.matmul(out=pb[:, :], lhsT=sonesr[:, :],
                                                        rhs=rinvb[:, :], start=True,
                                                        stop=True), inc=True)
                    p.c.wait(p.t, ntb)
                    p.c.wait(p.v, last_nvo[0])
                    ncb = p.c.op(lambda e: e.activation(out=rb[:, :], in_=pb[:, :],
                                                        func=AF.Identity), inc=True)
                    p.v.wait(p.c, ncb)
                    p.v.wait_chan(ch_oT)
                    nvo = None
                    for dr in range(2):
                        nvo = p.v.op(lambda e, dr=dr: e.tensor_tensor(
                            out=oT[:, dr * 512:(dr + 1) * 512], in0=po[dr][:, :],
                            in1=rb[:, :], op=ALU.mult), inc=True)
                    last_nvo[0] = nvo
                    p.g.wait(p.v, nvo)
                    if not nostore:
                        for dr in range(2):
                            p.g.dma(ch_oT,
                                    attnpre.ap()[h * DH + dr * 128: h * DH + dr * 128 + 128,
                                                 q0:q0 + 512],
                                    oT[:, dr * 512:(dr + 1) * 512])
                prev_attn[0] = p.v.n
                prev_attn[1] = p.c.n
        if dbg:
            p.y.wait(p.t, p.t.n)
            p.y.dma(ch_cst, dbgq.ap(), qT[:, :])
            p.y.dma(ch_cst, dbgk.ap(), kT[:, :])
            p.y.wait_chan(ch_cst)
        p.y.wait_chan(ch_oT)
        p.run()
    return nc



class Rot:

    def __init__(self, p, banks):
        self.p, self.banks = p, banks
        self.g = [None, None]
        self.i = 0

    def step(self, mms, evac_eng, evac_fn):
        b = self.banks[self.i % 2]
        if self.g[self.i % 2] is not None:
            eng, cnt = self.g[self.i % 2]
            self.p.t.wait(eng, cnt)
        tcnt = mms(b)
        evac_eng.wait(self.p.t, tcnt)
        cnt = evac_fn(b)
        self.g[self.i % 2] = (evac_eng, cnt)
        self.i += 1
        return cnt


def build_l1(ntt=8, detect=False):
    TT = 256
    NET = E // 128
    NKD = II // 128
    nc = bass.Bass("TRN2", target_bir_lowering=False, detect_race_conditions=detect)
    din = {}
    for name, shape, dt in [
        ("x", [TOWN, E], F32), ("xnT", [E, TOWN], BF16), ("apT", [E, TOWN], BF16),
        ("wo", [E, E], F32),
        ("a2d", [E, 64], F32), ("b2d", [64, E], F32), ("wrT", [E, 4], F32),
        ("r4", [4, 64], BF16),
        ("qb_qkv", [2048, E], U8), ("zb_qkv", [2048, 32], U8), ("sp_qkv", [E, 32], F32),
        ("qb_o", [2048, E], U8), ("zb_o", [2048, 32], U8), ("sp_o", [E, 32], F32),
        ("qb_gu", [II, E], U8), ("zb_gu", [II, 32], U8), ("sp_gu", [2 * II, 32], F32),
        ("qb_dn", [2048, II], U8), ("zb_dn", [2048, 86], U8), ("sp_dn", [E, 86], F32),
        ("q8f", [E, E], U8), ("fsc", [E, 32], F32), ("fz", [E, 32], F32),
        ("ln1c", [128, 32], F32), ("ln2c", [128, 32], F32), ("biasc", [128, 32], F32),
        ("punp", [512, 128], BF16), ("ident_bf", [128, 128], BF16),
        ("ones_col_bf", [128, 1], BF16), ("ones_row_bf", [1, 128], BF16),
    ]:
        din[name] = nc.dram_tensor(name, shape, dt, kind="ExternalInput").ap()
    out = nc.dram_tensor("out", [TOWN, E], F32, kind="ExternalOutput")

    pan = {
        "qkv": nc.dram_tensor("pan_qkv", [E, E], BF16, kind="Internal"),
        "o": nc.dram_tensor("pan_o", [E, E], BF16, kind="Internal"),
        "gu": nc.dram_tensor("pan_gu", [2 * II, E], BF16, kind="Internal"),
        "dn": nc.dram_tensor("pan_dn", [E, II], BF16, kind="Internal"),
        "ffn": nc.dram_tensor("pan_ffn", [E, E], BF16, kind="Internal"),
        "wo": nc.dram_tensor("pan_wo", [E, E], BF16, kind="Internal"),
    }

    with ExitStack() as ctx:
        en = ctx.enter_context
        sid = en(nc.sbuf_tensor("sid", [128, 128], BF16))
        spun = en(nc.sbuf_tensor("spun", [128, 512], BF16))
        a2d16 = en(nc.sbuf_tensor("a2d16", [128, NET * 64], BF16))
        b2d16 = en(nc.sbuf_tensor("b2d16", [64, E], BF16))
        wrT16 = en(nc.sbuf_tensor("wrT16", [128, NET * 4], BF16))
        r4b = en(nc.sbuf_tensor("r4b", [4, 64], BF16))
        ln1s = en(nc.sbuf_tensor("ln1s", [128, 32], F32))
        ln2s = en(nc.sbuf_tensor("ln2s", [128, 32], F32))
        bias_s = en(nc.sbuf_tensor("bias_s", [128, 32], F32))
        sonesc = en(nc.sbuf_tensor("sonesc", [128, 1], BF16))
        sonesr = en(nc.sbuf_tensor("sonesr", [1, 128], BF16))
        eps_ln = en(nc.sbuf_tensor("eps_ln", [128, 1], F32))
        eps_r1 = en(nc.sbuf_tensor("eps_r1", [128, 1], F32))
        eps_r2 = en(nc.sbuf_tensor("eps_r2", [1, 1], F32))
        st1 = en(nc.sbuf_tensor("st1", [128, 1], F32))
        st2 = en(nc.sbuf_tensor("st2", [128, 1], F32))
        st3 = en(nc.sbuf_tensor("st3", [128, 1], F32))
        st4 = en(nc.sbuf_tensor("st4", [128, 1], F32))
        p = Prog(nc, ctx)
        ch_c = p.chan("ch_c")

        p.y.dma(ch_c, sid[:, :], din["ident_bf"])
        p.y.dma(ch_c, spun[:, :], din["punp"].rearrange("(a p) c -> p a c", p=128))
        p.y.dma(ch_c, sonesc[:, :], din["ones_col_bf"])
        p.y.dma(ch_c, sonesr[:, :], din["ones_row_bf"])
        p.y.dma(ch_c, r4b[:, :], din["r4"])
        p.y.dma(ch_c, ln1s[:, :], din["ln1c"])
        p.y.dma(ch_c, ln2s[:, :], din["ln2c"])
        p.y.dma(ch_c, bias_s[:, :], din["biasc"])
        ch_cg = p.chan("ch_cg")
        p.g.dma(ch_cg, a2d16[:, :].rearrange("p (a d) -> p a d", d=64),
                din["a2d"].rearrange("(a p) d -> p a d", p=128))
        p.g.dma(ch_cg, b2d16[:, :], din["b2d"])
        p.g.dma(ch_cg, wrT16[:, :].rearrange("p (a d) -> p a d", d=4),
                din["wrT"].rearrange("(a p) d -> p a d", p=128))
        ge = p.g.op(lambda e: e.memset(eps_ln[:, :], 1e-5), inc=True)
        ge = p.g.op(lambda e: e.memset(eps_r1[:, :], 1e-6), inc=True)
        ge = p.g.op(lambda e: e.memset(eps_r2[:, :], 1e-6), inc=True)

        p.t.wait_chan(ch_c)
        with ExitStack() as dctx:
            de = dctx.enter_context
            ssp = de(nc.sbuf_tensor("ssp", [128, 5632], F32))
            zsn = de(nc.sbuf_tensor("zsn", [128, 5632], F32))
            szb = de(nc.sbuf_tensor("szb", [128, 2752], U8))
            sqb = [de(nc.sbuf_tensor(f"sqb{i}", [128, II], U8)) for i in range(2)]
            slo = de(nc.sbuf_tensor("slo", [128, II], U8))
            shi = de(nc.sbuf_tensor("shi", [128, II], U8))
            zlo = de(nc.sbuf_tensor("zlo", [128, 86], U8))
            zhi = de(nc.sbuf_tensor("zhi", [128, 86], U8))
            swrow = de(nc.sbuf_tensor("swrow", [128, II], BF16))
            stg = de(nc.sbuf_tensor("stg", [128, II], BF16))
            pdq = [de(nc.psum_tensor(f"pdq{i}", [128, 512], BF16)) for i in range(2)]
            ch_sp = p.chan("ch_sp")
            ch_qb = [p.chan("ch_qb0"), p.chan("ch_qb1")]
            ch_stg = p.chan("ch_stg")

            def dequant_packed(wname, K, N):
                nk = K // 128
                Kg = K // 128
                nbt = N // 256
                qb, zb, sp = din[f"qb_{wname}"], din[f"zb_{wname}"], din[f"sp_{wname}"]
                panel = pan[wname]
                p.y.wait(p.c, p.c.n)
                p.y.wait(p.v, p.v.n)
                p.y.dma(ch_sp, ssp[:, 0:(N // 128) * Kg].rearrange("p (a g) -> p a g", g=Kg),
                        sp.rearrange("(a p) g -> p a g", p=128))
                p.y.dma(ch_sp, szb[:, 0:(N // 256) * Kg].rearrange("p (a g) -> p a g", g=Kg),
                        zb.rearrange("(a p) g -> p a g", p=128))
                p.v.wait_chan(ch_sp)
                for bt in range(nbt):
                    zsl = szb[:, bt * Kg:(bt + 1) * Kg]
                    p.v.wait(p.v, p.v.n)
                    nv = p.v.op(lambda e, zsl=zsl, Kg=Kg: e.tensor_scalar(
                        out=zlo[:, 0:Kg], in0=zsl, scalar1=15, scalar2=None,
                        op0=ALU.bitwise_and), inc=True)
                    nv = p.v.op(lambda e, zsl=zsl, Kg=Kg: e.tensor_scalar(
                        out=zhi[:, 0:Kg], in0=zsl, scalar1=4, scalar2=15,
                        op0=ALU.logical_shift_right, op1=ALU.bitwise_and), inc=True)
                    p.v.wait(p.v, nv)
                    for half, zt in ((0, zlo), (1, zhi)):
                        nt = 2 * bt + half
                        nv = p.v.op(lambda e, zt=zt, nt=nt, Kg=Kg: e.scalar_tensor_tensor(
                            out=zsn[:, nt * Kg:(nt + 1) * Kg], in0=zt[:, 0:Kg],
                            scalar=-1.0, in1=ssp[:, nt * Kg:(nt + 1) * Kg],
                            op0=ALU.mult, op1=ALU.mult), inc=True)
                zsn_done = nv
                p.c.wait(p.v, zsn_done)
                rot = Rot(p, pdq)
                prev_aff = [0, 0]
                stg_pe = [0]
                for bt in range(nbt):
                    qch = ch_qb[bt % 2]
                    buf = sqb[bt % 2]
                    if prev_aff[bt % 2]:
                        p.y.wait(p.v, prev_aff[bt % 2])
                    p.y.dma(qch, buf[:, 0:K], qb[bt * 128:(bt + 1) * 128, :])
                    p.v.wait_chan(qch)
                    nv = p.v.op(lambda e, buf=buf, K=K: e.tensor_scalar(
                        out=slo[:, 0:K], in0=buf[:, 0:K], scalar1=15, scalar2=None,
                        op0=ALU.bitwise_and), inc=True)
                    nv = p.v.op(lambda e, buf=buf, K=K: e.tensor_scalar(
                        out=shi[:, 0:K], in0=buf[:, 0:K], scalar1=4, scalar2=15,
                        op0=ALU.logical_shift_right, op1=ALU.bitwise_and), inc=True)
                    prev_aff[bt % 2] = nv
                    p.c.wait(p.v, nv)
                    for half, nib in ((0, slo), (1, shi)):
                        nt = 2 * bt + half
                        p.c.wait(p.t, p.t.n)
                        ncx = None
                        for g in range(nk):
                            ncx = p.c.op(lambda e, nib=nib, g=g, nt=nt, Kg=Kg: e.activation(
                                out=swrow[:, g * 128:(g + 1) * 128],
                                in_=nib[:, g * 128:(g + 1) * 128], func=AF.Identity,
                                scale=ssp[:, nt * Kg + g:nt * Kg + g + 1],
                                bias=zsn[:, nt * Kg + g:nt * Kg + g + 1]), inc=(g == nk - 1))
                        p.t.wait(p.c, ncx)
                        p.v.wait_chan(ch_stg)
                        for gg in range((nk + 3) // 4):
                            def mms(b, gg=gg, nk=nk):
                                tv = None
                                for k4 in range(min(4, nk - gg * 4)):
                                    g = gg * 4 + k4
                                    tv = p.t.op(lambda e, b=b, k4=k4, g=g: e.transpose(
                                        out=b[:, k4 * 128:(k4 + 1) * 128],
                                        in_=swrow[:, g * 128:(g + 1) * 128],
                                        identity=sid[:, :]),
                                        inc=(k4 == min(4, nk - gg * 4) - 1))
                                return tv
                            def evac(b, gg=gg, nk=nk):
                                w = min(4, nk - gg * 4) * 128
                                return p.v.op(lambda e, b=b, gg=gg, w=w: e.tensor_copy(
                                    out=stg[:, gg * 512:gg * 512 + w], in_=b[:, 0:w]),
                                    inc=True)
                            rot.step(mms, p.v, evac)
                        p.y.wait(p.v, p.v.n)
                        p.y.dma(ch_stg, panel.ap()[nt * 128:(nt + 1) * 128, :],
                                stg[:, 0:nk * 128])

            dequant_packed("qkv", E, E)
            dequant_packed("o", E, E)
            dequant_packed("gu", E, 2 * II)
            dequant_packed("dn", II, E)

            p.y.wait(p.c, p.c.n)
            p.y.wait(p.v, p.v.n)
            p.y.dma(ch_sp, ssp[:, 0:32 * 32].rearrange("p (a g) -> p a g", g=32),
                    din["fsc"].rearrange("(a p) g -> p a g", p=128))
            p.y.dma(ch_sp, zsn[:, 0:32 * 32].rearrange("p (a g) -> p a g", g=32),
                    din["fz"].rearrange("(a p) g -> p a g", p=128))
            p.c.wait_chan(ch_sp)
            rot = Rot(p, pdq)
            prev_aff = [0, 0]
            for ot in range(NET):
                qch = ch_qb[ot % 2]
                buf = sqb[ot % 2]
                if prev_aff[ot % 2]:
                    p.y.wait(p.c, prev_aff[ot % 2])
                p.y.dma(qch, buf[:, 0:E], din["q8f"][ot * 128:(ot + 1) * 128, :])
                p.c.wait_chan(qch)
                p.c.wait(p.t, p.t.n)
                ncx = None
                for g in range(NET):
                    ncx = p.c.op(lambda e, buf=buf, g=g, ot=ot: e.activation(
                        out=swrow[:, g * 128:(g + 1) * 128],
                        in_=buf[:, g * 128:(g + 1) * 128], func=AF.Identity,
                        scale=ssp[:, ot * 32 + g:ot * 32 + g + 1],
                        bias=zsn[:, ot * 32 + g:ot * 32 + g + 1]), inc=(g == NET - 1))
                prev_aff[ot % 2] = ncx
                p.t.wait(p.c, ncx)
                p.v.wait_chan(ch_stg)
                for gg in range(NET // 4):
                    def mms(b, gg=gg):
                        tv = None
                        for k4 in range(4):
                            g = gg * 4 + k4
                            tv = p.t.op(lambda e, b=b, k4=k4, g=g: e.transpose(
                                out=b[:, k4 * 128:(k4 + 1) * 128],
                                in_=swrow[:, g * 128:(g + 1) * 128],
                                identity=sid[:, :]), inc=(k4 == 3))
                        return tv
                    def evac(b, gg=gg):
                        return p.v.op(lambda e, b=b, gg=gg: e.tensor_copy(
                            out=stg[:, gg * 512:(gg + 1) * 512], in_=b[:, :]), inc=True)
                    rot.step(mms, p.v, evac)
                p.y.wait(p.v, p.v.n)
                p.y.dma(ch_stg, pan["ffn"].ap()[ot * 128:(ot + 1) * 128, :],
                        stg[:, 0:E])

            ch_wog = p.chan("ch_wog")
            ch_woy = p.chan("ch_woy")
            p.g.wait_chan(ch_stg)
            p.g.wait(p.v, p.v.n)
            for ot in range(NET):
                p.g.wait_chan(ch_woy)
                p.g.dma(ch_wog, stg[:, 0:E].rearrange("p (a c) -> p a c", c=128),
                        din["wo"][:, ot * 128:(ot + 1) * 128]
                        .rearrange("(a p) c -> p a c", p=128))
                p.y.wait_chan(ch_wog)
                p.y.dma(ch_woy, pan["wo"].ap()[ot * 128:(ot + 1) * 128, :], stg[:, 0:E])
            p.y.wait_chan(ch_stg)
            p.y.wait_chan(ch_woy)

        B1 = en(nc.sbuf_tensor("B1", [128, 8192], BF16))
        B2 = en(nc.sbuf_tensor("B2", [128, 8192], BF16))
        B3 = en(nc.sbuf_tensor("B3", [128, 8192], BF16))
        B4 = en(nc.sbuf_tensor("B4", [128, 8192], BF16))
        B5 = en(nc.sbuf_tensor("B5", [128, 8192], BF16))
        B6 = en(nc.sbuf_tensor("B6", [128, 8192], BF16))
        B7 = en(nc.sbuf_tensor("B7", [128, 4096], BF16))
        B8 = en(nc.sbuf_tensor("B8", [128, 8192], BF16))
        B8f = B8.bitcast(F32)
        BM = en(nc.sbuf_tensor("BM", [128, 11520], BF16))
        BW = en(nc.sbuf_tensor("BW", [128, 11264], BF16))
        hl = en(nc.sbuf_tensor("hl", [128, 256], BF16))
        hh = en(nc.sbuf_tensor("hh", [128, 256], BF16))
        gb = en(nc.sbuf_tensor("gb", [128, 256], BF16))
        ub = en(nc.sbuf_tensor("ub", [128, 256], BF16))
        sg = en(nc.sbuf_tensor("sg", [128, 256], BF16))
        midb = en(nc.sbuf_tensor("midb", [64, 256], BF16))
        g64b = en(nc.sbuf_tensor("g64b", [64, 256], BF16))
        gmid = en(nc.sbuf_tensor("gmid", [64, 256], BF16))
        lg = en(nc.sbuf_tensor("lg", [128, 4], F32))
        lgt = en(nc.sbuf_tensor("lgt", [128, 8], F32))
        gates = en(nc.sbuf_tensor("gates", [128, 4], BF16))
        gT = en(nc.sbuf_tensor("gT", [4, 256], BF16))
        r2row = en(nc.sbuf_tensor("r2row", [1, 256], F32))
        r2rb = en(nc.sbuf_tensor("r2rb", [1, 256], BF16))
        R2b = en(nc.sbuf_tensor("R2b", [128, 256], BF16))
        sq2 = en(nc.sbuf_tensor("sq2", [128, 256], BF16))
        ffnb = en(nc.sbuf_tensor("ffnb", [128, 256], BF16))

        pP = [en(nc.psum_tensor(f"pP{i}", [128, 512], F32)) for i in range(2)]
        ptr = [en(nc.psum_tensor(f"ptr{i}", [128, 512], BF16)) for i in range(2)]
        psml = en(nc.psum_tensor("psml", [1, 512], F32))
        pm = en(nc.psum_tensor("pm", [64, 512], F32))
        pmsc = en(nc.psum_tensor("pmsc", [128, 512], F32))

        ch_b3 = p.chan("ch_b3")
        ch_sxn = p.chan("ch_sxn")
        ch_x = p.chan("ch_x")
        ch_pan = [p.chan("ch_pan0"), p.chan("ch_pan1")]
        ch_out = p.chan("ch_out")

        p.c.wait_chan(ch_c)
        p.v.wait_chan(ch_c)
        p.t.wait_chan(ch_c)
        p.t.wait_chan(ch_cg)
        p.g.wait(p.g, p.g.n)

        rotP = Rot(p, pP)
        rotT = Rot(p, ptr)
        b1_readers = [0]
        for it in range(ntt):
            t0 = it * TT
            p.y.wait(p.t, p.t.n)
            p.y.wait(p.v, p.v.n)
            p.y.dma(ch_b3, B3[:, 0:NET * TT].rearrange("p (a c) -> p a c", c=TT),
                    din["apT"][:, t0:t0 + TT].rearrange("(a p) t -> p a t", p=128))
            p.y.dma(ch_sxn, B1[:, 0:NET * TT].rearrange("p (a c) -> p a c", c=TT),
                    din["xnT"][:, t0:t0 + TT].rearrange("(a p) t -> p a t", p=128))
            p.t.wait_chan(ch_sxn)
            for ts in range(2):
                def mmsL(b, ts=ts):
                    tv = None
                    for et in range(NET):
                        tv = p.t.op(lambda e, b=b, et=et, ts=ts: e.matmul(
                            out=b[:, 0:4],
                            lhsT=B1[:, et * TT + ts * 128:et * TT + ts * 128 + 128],
                            rhs=wrT16[:, et * 4:(et + 1) * 4],
                            start=(et == 0), stop=(et == NET - 1)), inc=(et == NET - 1))
                    return tv
                def evacL(b, ts=ts):
                    nv = p.v.op(lambda e, b=b: e.tensor_copy(out=lg[:, :], in_=b[:, 0:4]),
                                inc=True)
                    p.v.wait(p.v, nv)
                    ops = [
                        lambda e: e.tensor_tensor(out=lgt[:, 0:1], in0=lg[:, 0:1], in1=lg[:, 1:2], op=ALU.max),
                        lambda e: e.tensor_tensor(out=lgt[:, 1:2], in0=lg[:, 0:1], in1=lg[:, 1:2], op=ALU.min),
                        lambda e: e.tensor_tensor(out=lgt[:, 2:3], in0=lg[:, 2:3], in1=lg[:, 3:4], op=ALU.max),
                        lambda e: e.tensor_tensor(out=lgt[:, 3:4], in0=lg[:, 2:3], in1=lg[:, 3:4], op=ALU.min),
                        lambda e: e.tensor_tensor(out=lgt[:, 4:5], in0=lgt[:, 0:1], in1=lgt[:, 2:3], op=ALU.max),
                        lambda e: e.tensor_tensor(out=lgt[:, 5:6], in0=lgt[:, 0:1], in1=lgt[:, 2:3], op=ALU.min),
                        lambda e: e.tensor_tensor(out=lgt[:, 6:7], in0=lgt[:, 1:2], in1=lgt[:, 3:4], op=ALU.max),
                        lambda e: e.tensor_tensor(out=lgt[:, 7:8], in0=lgt[:, 5:6], in1=lgt[:, 6:7], op=ALU.max),
                        lambda e: e.tensor_scalar(out=lgt[:, 4:5], in0=lgt[:, 4:5], scalar1=-1.0, scalar2=None, op0=ALU.mult),
                    ]
                    for f in ops:
                        nv = p.v.op(f, inc=True)
                        p.v.wait(p.v, nv)
                    p.c.wait(p.v, nv)
                    ncx = p.c.op(lambda e: e.activation(out=lg[:, :], in_=lg[:, :],
                                                        func=AF.Exp,
                                                        bias=lgt[:, 4:5]), inc=True)
                    p.v.wait(p.c, ncx)
                    nv = p.v.op(lambda e: e.reduce_max(out=lgt[:, 0:1], in_=lg[:, :],
                                                       axis=AX.X), inc=True)
                    p.v.wait(p.v, nv)
                    ops2 = [
                        lambda e: e.tensor_tensor(out=lgt[:, 1:2], in0=lg[:, 0:1], in1=lg[:, 1:2], op=ALU.max),
                        lambda e: e.tensor_tensor(out=lgt[:, 2:3], in0=lg[:, 0:1], in1=lg[:, 1:2], op=ALU.min),
                        lambda e: e.tensor_tensor(out=lgt[:, 3:4], in0=lg[:, 2:3], in1=lg[:, 3:4], op=ALU.max),
                        lambda e: e.tensor_tensor(out=lgt[:, 5:6], in0=lg[:, 2:3], in1=lg[:, 3:4], op=ALU.min),
                        lambda e: e.tensor_tensor(out=lgt[:, 6:7], in0=lgt[:, 1:2], in1=lgt[:, 3:4], op=ALU.min),
                        lambda e: e.tensor_tensor(out=lgt[:, 2:3], in0=lgt[:, 2:3], in1=lgt[:, 5:6], op=ALU.max),
                        lambda e: e.tensor_tensor(out=lgt[:, 7:8], in0=lgt[:, 6:7], in1=lgt[:, 2:3], op=ALU.max),
                    ]
                    for f in ops2:
                        nv = p.v.op(f, inc=True)
                        p.v.wait(p.v, nv)
                    nv = p.v.op(lambda e: e.tensor_scalar(
                        out=lgt[:, 0:4], in0=lg[:, :], scalar1=lgt[:, 7:8], scalar2=None,
                        op0=ALU.is_ge), inc=True)
                    p.v.wait(p.v, nv)
                    nv = p.v.op(lambda e: e.tensor_tensor(
                        out=lg[:, :], in0=lg[:, :], in1=lgt[:, 0:4], op=ALU.mult), inc=True)
                    p.v.wait(p.v, nv)
                    nv = p.v.op(lambda e: e.reduce_sum(out=lgt[:, 5:6], in_=lg[:, :],
                                                       axis=AX.X), inc=True)
                    p.v.wait(p.v, nv)
                    nv = p.v.op(lambda e: e.reciprocal(out=lgt[:, 6:7], in_=lgt[:, 5:6]),
                                inc=True)
                    p.v.wait(p.v, nv)
                    nv = p.v.op(lambda e: e.tensor_scalar(
                        out=gates[:, :], in0=lg[:, :], scalar1=lgt[:, 6:7], scalar2=4.0,
                        op0=ALU.mult, op1=ALU.mult), inc=True)
                    return nv
                nvg = rotP.step(mmsL, p.v, evacL)
                p.t.wait(p.v, nvg)
                def mmsG(b, ts=ts):
                    return p.t.op(lambda e, b=b: e.transpose(
                        out=b[:, 0:4].bitcast(BF16)[:, 0:4] if False else b[:, 0:4],
                        in_=gates[:, :], identity=sid[:, :]), inc=True)
                def mmsG2(b, ts=ts):
                    return p.t.op(lambda e, b=b: e.transpose(
                        out=b[0:4, 0:128], in_=gates[:, :], identity=sid[:, :]), inc=True)
                def evacG(b, ts=ts):
                    return p.v.op(lambda e, b=b, ts=ts: e.tensor_copy(
                        out=gT[:, ts * 128:(ts + 1) * 128], in_=b[0:4, 0:128]), inc=True)
                rotT.step(mmsG2, p.v, evacG)
            def mmsM(b):
                tv = None
                for et in range(NET):
                    tv = p.t.op(lambda e, b=b, et=et: e.matmul(
                        out=b[0:64, 0:TT], lhsT=a2d16[:, et * 64:(et + 1) * 64],
                        rhs=B1[:, et * TT:(et + 1) * TT],
                        start=(et == 0), stop=(et == NET - 1)), inc=(et == NET - 1))
                return tv
            def evacM(b):
                return p.c.op(lambda e, b=b: e.activation(
                    out=midb[:, :], in_=b[0:64, 0:TT], func=AF.Identity), inc=True)
            p.t.wait(p.v, p.v.n)
            p.t.wait(p.c, p.c.n)
            tvm = mmsM(pm)
            p.c.wait(p.t, tvm)
            ncm = evacM(pm)
            p.t.wait(p.c, ncm)
            p.t.wait(p.v, p.v.n)
            tvg = p.t.op(lambda e: e.matmul(out=pm[0:64, 0:TT], lhsT=r4b[:, :],
                                            rhs=gT[:, :], start=True, stop=True), inc=True)
            p.c.wait(p.t, tvg)
            ncg = p.c.op(lambda e: e.activation(out=g64b[:, :], in_=pm[0:64, 0:TT],
                                                func=AF.Identity), inc=True)
            p.v.wait(p.c, ncg)
            nvq = p.v.op(lambda e: e.tensor_tensor(out=gmid[:, :], in0=midb[:, :],
                                                   in1=g64b[:, :], op=ALU.mult), inc=True)
            p.t.wait_chan(ch_b3)
            p.t.wait(p.v, nvq)
            for ot in range(NET):
                pch = ch_pan[ot % 2]
                p.y.wait(p.t, p.t.n)
                wslice = BW[:, (ot % 2) * 4096:(ot % 2) * 4096 + 4096]
                p.y.dma(pch, wslice, pan["wo"].ap()[ot * 128:(ot + 1) * 128, :])
                p.t.wait_chan(pch)
                def mmsP(b, ot=ot, wslice=wslice):
                    for et in range(NET):
                        p.t.op(lambda e, b=b, et=et, wslice=wslice: e.matmul(
                            out=b[:, 0:TT], lhsT=wslice[:, et * 128:(et + 1) * 128],
                            rhs=B3[:, et * TT:(et + 1) * TT],
                            start=(et == 0), stop=False))
                    return p.t.op(lambda e, b=b, ot=ot: e.matmul(
                        out=b[:, 0:TT], lhsT=b2d16[:, ot * 128:(ot + 1) * 128],
                        rhs=gmid[:, :], start=False, stop=True), inc=True)
                def evacP(b, ot=ot):
                    return p.c.op(lambda e, b=b, ot=ot: e.activation(
                        out=B1[:, ot * TT:(ot + 1) * TT], in_=b[:, 0:TT],
                        func=AF.Identity), inc=True)
                rotP.step(mmsP, p.c, evacP)
            d0T_done = p.c.n
            for ts in range(2):
                p.g.wait(p.v, p.v.n)
                p.g.dma(ch_x, B7[:, :],
                        din["x"][t0 + ts * 128:t0 + ts * 128 + 128, :])
                p.v.wait_chan(ch_x)
                p.v.wait(p.c, d0T_done)
                for gg in range(8):
                    def mmsX(b, gg=gg, ts=ts):
                        tv = None
                        for k4 in range(4):
                            ot = gg * 4 + k4
                            tv = p.t.op(lambda e, b=b, k4=k4, ot=ot, ts=ts: e.transpose(
                                out=b[:, k4 * 128:(k4 + 1) * 128],
                                in_=B1[:, ot * TT + ts * 128:ot * TT + ts * 128 + 128],
                                identity=sid[:, :]), inc=(k4 == 3))
                        return tv
                    def evacX(b, gg=gg, ts=ts):
                        return p.v.op(lambda e, b=b, gg=gg, ts=ts: e.tensor_tensor(
                            out=B6[:, ts * 4096 + gg * 512:ts * 4096 + (gg + 1) * 512],
                            in0=b[:, :], in1=B7[:, gg * 512:(gg + 1) * 512],
                            op=ALU.add), inc=True)
                    rotT.step(mmsX, p.v, evacX)
            for ts in range(2):
                x1c = B6[:, ts * 4096:(ts + 1) * 4096]
                nv = p.v.op(lambda e, x1c=x1c: e.reduce_sum(out=st1[:, :], in_=x1c,
                                                            axis=AX.X), inc=True)
                p.v.wait(p.v, nv)
                nv = p.v.op(lambda e: e.tensor_scalar(out=st1[:, :], in0=st1[:, :],
                                                      scalar1=1.0 / E, scalar2=None,
                                                      op0=ALU.mult), inc=True)
                p.v.wait(p.v, nv)
                p.c.wait(p.v, nv)
                if ts == 0:
                    p.c.wait(p.t, p.t.n)
                ncx = p.c.op(lambda e, x1c=x1c: e.activation(
                    out=B8[:, 0:4096], in_=x1c, func=AF.Square,
                    accum_out=st2[:, :]), inc=True)
                p.c.wait(p.c, ncx)
                nv = p.v.op(lambda e: e.tensor_scalar(
                    out=st3[:, :], in0=st1[:, :], scalar1=st1[:, :], scalar2=None,
                    op0=ALU.mult), inc=True)
                p.v.wait(p.v, nv)
                p.v.wait(p.c, ncx)
                nv = p.v.op(lambda e: e.tensor_scalar(
                    out=st2[:, :], in0=st2[:, :], scalar1=1.0 / E, scalar2=st3[:, :],
                    op0=ALU.mult, op1=ALU.subtract), inc=True)
                p.c.wait(p.v, nv)
                ncx = p.c.op(lambda e: e.activation(out=st3[:, :], in_=st2[:, :],
                                                    func=AF.Ln, bias=eps_ln[:, :]),
                             inc=True)
                p.c.wait(p.c, ncx)
                ncx = p.c.op(lambda e: e.activation(out=st3[:, :], in_=st3[:, :],
                                                    func=AF.Exp, scale=-0.5), inc=True)
                p.v.wait(p.c, ncx)
                nv = p.v.op(lambda e: e.tensor_scalar(
                    out=st4[:, :], in0=st3[:, :], scalar1=st1[:, :], scalar2=-1.0,
                    op0=ALU.mult, op1=ALU.mult), inc=True)
                p.v.wait(p.v, nv)
                nv = p.v.op(lambda e, x1c=x1c: e.tensor_scalar(
                    out=B8[:, 0:4096], in0=x1c, scalar1=st3[:, :], scalar2=st4[:, :],
                    op0=ALU.mult, op1=ALU.add), inc=True)
                p.c.wait(p.v, nv)
                ncx = p.c.op(lambda e: e.activation(
                    out=B8[:, 4096:8192], in_=B8[:, 0:4096], func=AF.Square,
                    accum_out=st2[:, :]), inc=True)
                p.c.wait(p.c, ncx)
                ncx = p.c.op(lambda e: e.activation(out=st1[:, :], in_=st2[:, :],
                                                    func=AF.Ln, scale=1.0 / E,
                                                    bias=eps_r1[:, :]), inc=True)
                p.c.wait(p.c, ncx)
                ncx = p.c.op(lambda e: e.activation(out=st1[:, :], in_=st1[:, :],
                                                    func=AF.Exp, scale=-0.5), inc=True)
                p.v.wait(p.c, ncx)
                nv = p.v.op(lambda e: e.tensor_scalar(
                    out=B8[:, 4096:8192], in0=B8[:, 0:4096], scalar1=st1[:, :],
                    scalar2=None, op0=ALU.mult), inc=True)
                g1r_done = nv
                p.t.wait(p.v, g1r_done)
                for gg in range(8):
                    def mmsT(b, gg=gg, ts=ts):
                        tv = None
                        for k4 in range(4):
                            et = gg * 4 + k4
                            tv = p.t.op(lambda e, b=b, k4=k4, et=et: e.transpose(
                                out=b[:, k4 * 128:(k4 + 1) * 128],
                                in_=B8[:, et * 128:(et + 1) * 128],
                                identity=sid[:, :]), inc=(k4 == 3))
                        return tv
                    def evacT(b, gg=gg, ts=ts):
                        nv = None
                        for k4 in range(4):
                            et = gg * 4 + k4
                            nv = p.v.op(lambda e, b=b, k4=k4, et=et, ts=ts: e.tensor_copy(
                                out=B4[:, et * TT + ts * 128:et * TT + ts * 128 + 128],
                                in_=b[:, k4 * 128:(k4 + 1) * 128]), inc=(k4 == 3))
                        return nv
                    rotT.step(mmsT, p.v, evacT)
                for gg in range(8):
                    def mmsU(b, gg=gg, ts=ts):
                        tv = None
                        for k4 in range(4):
                            et = gg * 4 + k4
                            tv = p.t.op(lambda e, b=b, k4=k4, et=et: e.transpose(
                                out=b[:, k4 * 128:(k4 + 1) * 128],
                                in_=B8[:, 4096 + et * 128:4096 + (et + 1) * 128],
                                identity=sid[:, :]), inc=(k4 == 3))
                        return tv
                    def evacU(b, gg=gg, ts=ts):
                        nv = None
                        for k4 in range(4):
                            et = gg * 4 + k4
                            nv = p.v.op(lambda e, b=b, k4=k4, et=et, ts=ts: e.tensor_scalar(
                                out=B1[:, et * TT + ts * 128:et * TT + ts * 128 + 128],
                                in0=b[:, k4 * 128:(k4 + 1) * 128],
                                scalar1=ln1s[:, et:et + 1], scalar2=None,
                                op0=ALU.mult), inc=(k4 == 3))
                        return nv
                    rotT.step(mmsU, p.v, evacU)
            p.t.wait(p.v, p.v.n)
            def quant_mm(wname, nk, rhs_buf, nt, b):
                pch = ch_pan[nt % 2]
                p.y.wait(p.t, p.t.n)
                w = BW[:, (nt % 2) * 4096:(nt % 2) * 4096 + nk * 128] if nk <= 32 else BW[:, 0:nk * 128]
                p.y.dma(pch, w, pan[wname].ap()[nt * 128:(nt + 1) * 128, :])
                p.t.wait_chan(pch)
                tv = None
                for kt in range(nk):
                    tv = p.t.op(lambda e, b=b, kt=kt, w=w, rhs_buf=rhs_buf: e.matmul(
                        out=b[:, 0:TT], lhsT=w[:, kt * 128:(kt + 1) * 128],
                        rhs=rhs_buf[:, kt * TT:(kt + 1) * TT],
                        start=(kt == 0), stop=(kt == nk - 1)), inc=(kt == nk - 1))
                return tv

            for wname, rhs_buf, dst, addres in (("qkv", B1, B2, None),
                                                ("o", B2, B5, B4)):
                p.t.wait(p.c, p.c.n)
                p.t.wait(p.v, p.v.n)
                for bt in range(16):
                    rotP.step(lambda b, bt=bt, wname=wname, rhs_buf=rhs_buf:
                              quant_mm(wname, NET, rhs_buf, 2 * bt, b),
                              p.c,
                              lambda b: p.c.op(lambda e, b=b: e.activation(
                                  out=hl[:, :], in_=b[:, 0:TT], func=AF.Identity),
                                  inc=True))
                    rotP.step(lambda b, bt=bt, wname=wname, rhs_buf=rhs_buf:
                              quant_mm(wname, NET, rhs_buf, 2 * bt + 1, b),
                              p.c,
                              lambda b: p.c.op(lambda e, b=b: e.activation(
                                  out=hh[:, :], in_=b[:, 0:TT], func=AF.Identity),
                                  inc=True))
                    p.t.wait(p.c, p.c.n)
                    for sub in range(2):
                        def mmsN(b, sub=sub):
                            p.t.op(lambda e, b=b, sub=sub: e.matmul(
                                out=b[:, 0:TT], lhsT=spun[:, (2 * sub) * 128:(2 * sub + 1) * 128],
                                rhs=hl[:, :], start=True, stop=False))
                            return p.t.op(lambda e, b=b, sub=sub: e.matmul(
                                out=b[:, 0:TT], lhsT=spun[:, (2 * sub + 1) * 128:(2 * sub + 2) * 128],
                                rhs=hh[:, :], start=False, stop=True), inc=True)
                        nt_out = 2 * bt + sub
                        if addres is None:
                            def evacN(b, nt_out=nt_out, dst=dst):
                                return p.c.op(lambda e, b=b, nt_out=nt_out: e.activation(
                                    out=dst[:, nt_out * TT:(nt_out + 1) * TT],
                                    in_=b[:, 0:TT], func=AF.Identity), inc=True)
                            rotP.step(mmsN, p.c, evacN)
                        else:
                            def evacN(b, nt_out=nt_out, dst=dst, addres=addres):
                                return p.v.op(lambda e, b=b, nt_out=nt_out: e.tensor_tensor(
                                    out=dst[:, nt_out * TT:(nt_out + 1) * TT],
                                    in0=b[:, 0:TT],
                                    in1=addres[:, nt_out * TT:(nt_out + 1) * TT],
                                    op=ALU.add), inc=True)
                            rotP.step(mmsN, p.v, evacN)
            p.c.wait(p.v, p.v.n)
            sq_done = None
            for et in range(NET):
                if sq_done is None:
                    p.c.wait(p.t, p.t.n)
                else:
                    p.c.wait(p.t, sq_done)
                ncx = p.c.op(lambda e, et=et: e.activation(
                    out=sq2[:, :], in_=B5[:, et * TT:(et + 1) * TT], func=AF.Square),
                    inc=True)
                p.t.wait(p.c, ncx)
                sq_done = p.t.op(lambda e, et=et: e.matmul(
                    out=psml[:, 0:TT], lhsT=sonesc[:, :], rhs=sq2[:, :],
                    start=(et == 0), stop=(et == NET - 1)), inc=True)
            p.c.wait(p.t, sq_done)
            ncx = p.c.op(lambda e: e.activation(out=r2row[:, :], in_=psml[0:1, 0:TT],
                                                func=AF.Ln, scale=1.0 / E,
                                                bias=eps_r2[:, :]), inc=True)
            p.c.wait(p.c, ncx)
            ncx = p.c.op(lambda e: e.activation(out=r2row[:, :], in_=r2row[:, :],
                                                func=AF.Exp, scale=-0.5), inc=True)
            p.v.wait(p.c, ncx)
            nv = p.v.op(lambda e: e.tensor_copy(out=r2rb[:, :], in_=r2row[:, :]), inc=True)
            p.t.wait(p.v, nv)
            tvb = p.t.op(lambda e: e.matmul(out=pmsc[:, 0:TT], lhsT=sonesr[:, :],
                                            rhs=r2rb[:, :], start=True, stop=True),
                         inc=True)
            p.c.wait(p.t, tvb)
            ncx = p.c.op(lambda e: e.activation(out=R2b[:, :], in_=pmsc[:, 0:TT],
                                                func=AF.Identity), inc=True)
            p.v.wait(p.c, ncx)
            nvg2 = None
            for et in range(NET):
                nvg2 = p.v.op(lambda e, et=et: e.scalar_tensor_tensor(
                    out=B1[:, et * TT:(et + 1) * TT], in0=B5[:, et * TT:(et + 1) * TT],
                    scalar=ln2s[:, et:et + 1], in1=R2b[:, :],
                    op0=ALU.mult, op1=ALU.mult), inc=(et == NET - 1))
            p.t.wait(p.v, nvg2)
            for half, bts in ((0, range(0, 22)), (1, range(22, 43))):
                nmk = len(bts) * 2
                mbase = bts[0] * 2
                prev_mv = [0, 0]
                for bt in bts:
                    for sub in range(2):
                        gpt = 2 * bt + sub
                        upt = gpt + II // 128
                        def evacGb(b):
                            if prev_mv[0]:
                                p.c.wait(p.v, prev_mv[0])
                            return p.c.op(lambda e, b=b: e.activation(
                                out=gb[:, :], in_=b[:, 0:TT], func=AF.Silu), inc=True)
                        rotP.step(lambda b, gpt=gpt: quant_mm("gu", NET, B1, gpt, b),
                                  p.c, evacGb)
                        tgt = hl if sub == 0 else hh
                        def evacUb(b, tgt=tgt):
                            ncx = p.c.op(lambda e, b=b: e.activation(
                                out=ub[:, :], in_=b[:, 0:TT], func=AF.Identity), inc=True)
                            p.v.wait(p.c, ncx)
                            nv = p.v.op(lambda e, tgt=tgt: e.tensor_tensor(
                                out=tgt[:, :], in0=gb[:, :], in1=ub[:, :], op=ALU.mult),
                                inc=True)
                            prev_mv[0] = nv
                            return ncx
                        rotP.step(lambda b, upt=upt: quant_mm("gu", NET, B1, upt, b),
                                  p.c, evacUb)
                    p.t.wait(p.v, prev_mv[0])
                    for sub in range(2):
                        def mmsN2(b, sub=sub):
                            p.t.op(lambda e, b=b, sub=sub: e.matmul(
                                out=b[:, 0:TT], lhsT=spun[:, (2 * sub) * 128:(2 * sub + 1) * 128],
                                rhs=hl[:, :], start=True, stop=False))
                            return p.t.op(lambda e, b=b, sub=sub: e.matmul(
                                out=b[:, 0:TT], lhsT=spun[:, (2 * sub + 1) * 128:(2 * sub + 2) * 128],
                                rhs=hh[:, :], start=False, stop=True), inc=True)
                        mk = 2 * bt + sub - mbase
                        def evacN2(b, mk=mk):
                            return p.c.op(lambda e, b=b, mk=mk: e.activation(
                                out=BM[:, mk * TT:(mk + 1) * TT], in_=b[:, 0:TT],
                                func=AF.Identity), inc=True)
                        rotP.step(mmsN2, p.c, evacN2)
                p.t.wait(p.c, p.c.n)
                for bt2 in range(16):
                    for sub in range(2):
                        nt = 2 * bt2 + sub
                        def mmsD(b, nt=nt, nmk=nmk, mbase=mbase):
                            pch = ch_pan[nt % 2]
                            p.y.wait(p.t, p.t.n)
                            wdn = BW[:, 0:nmk * 128]
                            p.y.dma(pch, wdn,
                                    pan["dn"].ap()[nt * 128:(nt + 1) * 128,
                                                   mbase * 128:(mbase + nmk) * 128])
                            p.t.wait_chan(pch)
                            tv = None
                            for kt in range(nmk):
                                tv = p.t.op(lambda e, b=b, kt=kt, wdn=wdn: e.matmul(
                                    out=b[:, 0:TT], lhsT=wdn[:, kt * 128:(kt + 1) * 128],
                                    rhs=BM[:, kt * TT:(kt + 1) * TT],
                                    start=(kt == 0), stop=(kt == nmk - 1)),
                                    inc=(kt == nmk - 1))
                            return tv
                        tgt2 = hl if sub == 0 else hh
                        def evacD(b, tgt2=tgt2):
                            return p.c.op(lambda e, b=b, tgt2=tgt2: e.activation(
                                out=tgt2[:, :], in_=b[:, 0:TT], func=AF.Identity),
                                inc=True)
                        rotP.step(mmsD, p.c, evacD)
                    p.t.wait(p.c, p.c.n)
                    for sub in range(2):
                        def mmsN3(b, sub=sub):
                            p.t.op(lambda e, b=b, sub=sub: e.matmul(
                                out=b[:, 0:TT], lhsT=spun[:, (2 * sub) * 128:(2 * sub + 1) * 128],
                                rhs=hl[:, :], start=True, stop=False))
                            return p.t.op(lambda e, b=b, sub=sub: e.matmul(
                                out=b[:, 0:TT], lhsT=spun[:, (2 * sub + 1) * 128:(2 * sub + 2) * 128],
                                rhs=hh[:, :], start=False, stop=True), inc=True)
                        nt_out = 2 * bt2 + sub
                        if half == 0:
                            def evacN3(b, nt_out=nt_out):
                                return p.v.op(lambda e, b=b, nt_out=nt_out: e.tensor_tensor(
                                    out=B2[:, nt_out * TT:(nt_out + 1) * TT],
                                    in0=b[:, 0:TT],
                                    in1=B5[:, nt_out * TT:(nt_out + 1) * TT],
                                    op=ALU.add), inc=True)
                        else:
                            def evacN3(b, nt_out=nt_out):
                                return p.v.op(lambda e, b=b, nt_out=nt_out: e.tensor_tensor(
                                    out=B2[:, nt_out * TT:(nt_out + 1) * TT],
                                    in0=b[:, 0:TT],
                                    in1=B2[:, nt_out * TT:(nt_out + 1) * TT],
                                    op=ALU.add), inc=True)
                        rotP.step(mmsN3, p.v, evacN3)
            prev_nvf = [0]
            for ot in range(NET):
                def mmsFF(b, ot=ot):
                    return quant_mm("ffn", NET, B4, ot, b)
                def evacFF(b, ot=ot):
                    if prev_nvf[0]:
                        p.c.wait(p.v, prev_nvf[0])
                    ncf = p.c.op(lambda e, b=b, ot=ot: e.activation(
                        out=ffnb[:, :], in_=b[:, 0:TT], func=AF.Identity,
                        bias=bias_s[:, ot:ot + 1]), inc=True)
                    p.v.wait(p.c, ncf)
                    nvf = p.v.op(lambda e, ot=ot: e.tensor_tensor(
                        out=B2[:, ot * TT:(ot + 1) * TT],
                        in0=B2[:, ot * TT:(ot + 1) * TT],
                        in1=ffnb[:, :], op=ALU.add), inc=True)
                    prev_nvf[0] = nvf
                    return ncf
                rotP.step(mmsFF, p.c, evacFF)
            for ts in range(2):
                for gg in range(8):
                    def mmsF(b, gg=gg, ts=ts):
                        tv = None
                        for k4 in range(4):
                            ot = gg * 4 + k4
                            tv = p.t.op(lambda e, b=b, k4=k4, ot=ot, ts=ts: e.transpose(
                                out=b[:, k4 * 128:(k4 + 1) * 128],
                                in_=B2[:, ot * TT + ts * 128:ot * TT + ts * 128 + 128],
                                identity=sid[:, :]), inc=(k4 == 3))
                        return tv
                    def evacF(b, gg=gg, ts=ts):
                        return p.v.op(lambda e, b=b, gg=gg, ts=ts: e.tensor_tensor(
                            out=B8f[:, gg * 512:(gg + 1) * 512], in0=b[:, :],
                            in1=B6[:, ts * 4096 + gg * 512:ts * 4096 + (gg + 1) * 512],
                            op=ALU.add), inc=True)
                    rotT.step(mmsF, p.v, evacF)
                p.y.wait(p.v, p.v.n)
                p.y.dma(ch_out, out.ap()[t0 + ts * 128:t0 + ts * 128 + 128, :],
                        B8f[:, 0:4096])
                p.v.wait_chan(ch_out)
        p.y.wait_chan(ch_out)
        p.run()
    return nc


def _host_prep(inputs):
    d = inputs
    consts = _consts()
    om4096 = _orig_map(4096)
    om22016 = _orig_map(2 * II)
    g = {}
    g["consts"] = consts
    g["l1_common"] = {
        "wo": d["L0_attn_Wo"],
        "a2d": np.ascontiguousarray(d["L0_ffn_A"].transpose(2, 0, 1).reshape(E, 64)),
        "b2d": np.ascontiguousarray(d["L0_ffn_B"].transpose(0, 2, 1).reshape(64, E)),
        "wrT": np.ascontiguousarray(d["L0_ffn_Wr"].T),
        "r4": consts["r4"],
        "qb_qkv": _pack_bytes(d["L1_attn_qkv_qw"][:, :512]),
        "zb_qkv": _pack_bytes(d["L1_attn_qkv_qz"][:, :512]),
        "sp_qkv": np.ascontiguousarray(d["L1_attn_qkv_sc"][:, :4096].T[om4096]),
        "qb_o": _pack_bytes(d["L1_attn_o_qw"]),
        "zb_o": _pack_bytes(d["L1_attn_o_qz"]),
        "sp_o": np.ascontiguousarray(d["L1_attn_o_sc"].T[om4096]),
        "qb_gu": _pack_bytes(d["L1_attn_gu_qw"]),
        "zb_gu": _pack_bytes(d["L1_attn_gu_qz"]),
        "sp_gu": np.ascontiguousarray(d["L1_attn_gu_sc"].T[om22016]),
        "qb_dn": _pack_bytes(d["L1_attn_dn_qw"]),
        "zb_dn": _pack_bytes(d["L1_attn_dn_qz"]),
        "sp_dn": np.ascontiguousarray(d["L1_attn_dn_sc"].T[om4096]),
        "q8f": np.ascontiguousarray(
            d["L1_ffn_qweight"].view("<u1").reshape(E, 32, 128, 4)[:, :, :, 0]
            .reshape(E, E)),
        "fsc": np.ascontiguousarray(d["L1_ffn_scales"].reshape(E, 32)),
        "fz": np.ascontiguousarray(d["L1_ffn_scaled_zeros"].reshape(E, 32)),
        "ln1c": np.ascontiguousarray(d["L1_attn_ln1_w"].reshape(32, 128).T),
        "ln2c": np.ascontiguousarray(d["L1_attn_ln2_w"].reshape(32, 128).T),
        "biasc": np.ascontiguousarray(d["L1_ffn_bias"].reshape(32, 128).T),
        "punp": consts["punp"],
        "ident_bf": consts["ident_bf"],
        "ones_col_bf": consts["ones_col_bf"],
        "ones_row_bf": np.ones((1, 128), dtype=BF),
    }
    return g


def kernel(**inputs):
    x = inputs["L0_attn_hidden_states"]
    g = _host_prep(inputs)
    consts = g["consts"]
    cores = list(range(NC))

    nc1 = build_ln(TOWN)
    im1 = []
    for c in cores:
        b, t0 = c // 4, (c % 4) * TOWN
        im1.append({"x": np.ascontiguousarray(x[b, t0:t0 + TOWN]),
                    "ident_bf": consts["ident_bf"]})
    r1 = run_bass_kernel_spmd(nc1, im1, core_ids=cores)
    xnT_sh = [r1.results[c]["xnT"] for c in cores]
    xnT_b = [np.concatenate([xnT_sh[4 * b + j] for j in range(4)], axis=1)
             for b in range(NB)]

    nc2 = build_attn()
    im2 = []
    for c in cores:
        b, hg = c // 4, c % 4
        im2.append({
            "xnT": xnT_b[b],
            "wq": np.ascontiguousarray(inputs["L0_attn_Wq"][:, hg * 1024:(hg + 1) * 1024]),
            "wk": np.ascontiguousarray(inputs["L0_attn_Wk"][:, hg * 1024:(hg + 1) * 1024]),
            "wv": np.ascontiguousarray(inputs["L0_attn_Wv"][:, hg * 1024:(hg + 1) * 1024]),
            "rsincos": consts["rsincos"], "masks": consts["masks"],
            "pswap": consts["pswap"], "ones_col_bf": consts["ones_col_bf"],
            "ones_row_f": consts["ones_row_f"],
        })
    r2 = run_bass_kernel_spmd(nc2, im2, core_ids=cores)
    apT_b = []
    for b in range(NB):
        apT_b.append(np.concatenate(
            [r2.results[4 * b + j]["attnpre"] for j in range(4)], axis=0))

    nc3 = build_l1(ntt=TOWN // 256)
    im3 = []
    for c in cores:
        b, t0 = c // 4, (c % 4) * TOWN
        im = dict(g["l1_common"])
        im["x"] = np.ascontiguousarray(x[b, t0:t0 + TOWN])
        im["xnT"] = np.ascontiguousarray(xnT_sh[c])
        im["apT"] = np.ascontiguousarray(apT_b[b][:, t0:t0 + TOWN])
        im3.append(im)
    r3 = run_bass_kernel_spmd(nc3, im3, core_ids=cores)
    out = np.empty((NB, S, E), dtype=np.float32)
    for c in cores:
        b, t0 = c // 4, (c % 4) * TOWN
        out[b, t0:t0 + TOWN] = r3.results[c]["out"]
    return out
